# version 18
# baseline (speedup 1.0000x reference)
import numpy as np
import concourse.bass as bass
import concourse.mybir as mybir
from concourse.bass_utils import run_bass_kernel_spmd

F32 = mybir.dt.float32
AF = mybir.ActivationFunctionType
ALU = mybir.AluOpType

W_WIN = 50
NT = 2048 + 2 * (W_WIN + 1)
C0, C1 = W_WIN + 1, W_WIN + 1 + 2048
D, DC = 1024, 8
H, HC = 4096, 32
FTW = NT // 5
assert FTW * 5 == NT and FTW % 2 == 0 and FTW <= 512
FT = [(i * FTW, (i + 1) * FTW) for i in range(5)]
BT = [(C0 + i * 512, C0 + (i + 1) * 512) for i in range(4)]
PW = [(1 + i * FTW, min(1 + (i + 1) * FTW, NT - 1)) for i in range(5)]

POOL_W = 35000
MASKBC = 26784
DATA1 = NT
ABARBC = 2 * NT
HS = 3 * NT
YS = 4 * NT
P1_LNX = 0
P1_HT = P1_LNX + 2 * 8 * FTW
P1_WB1 = P1_HT + 32 * FTW
P1_WB2 = P1_WB1 + 2048
P1_SQ = P1_WB2 + 2048
P1_BT = P1_SQ + 1024
P1_RW = P1_BT + 1024
P2_TN = NT
P2_WBI = P2_TN + 8 * NT
P2_TMP = P2_WBI + 2048
P2_SCR = P2_TMP + 1024
P2_SQ2 = 24736
P2_RW = 29856
P2_IPW0 = 33440
P3_PWW = HS
P3_WIN = P3_PWW + 8192
P3_XS = P3_WIN + 1024
P3_XC = P3_XS + 4 * (FTW + 4)
P3_XP = P3_XC + 4 * (FTW + 4)
P5_YG = YS + 2048 + 64
P5_GT = P5_YG + 4096
P5_OW = P5_GT + 1024 + 64
P5_WOC = 33792
P5_TMP = P5_OW
P6_LNX, P6_HT, P6_WB1, P6_WB2, P6_SQ, P6_BT, P6_RW = 0, 8192, 24576, 26624, 28672, 29696, 30720

SB1, SB2, SINB, SPWB, SOUTB, SDW, SDWB, SNS = 0, 32, 40, 56, 64, 72, 96, 104
SLOGA, SBP, SCP, SLOGDT = 112, 113, 114, 115
SB2H = 116
SPRM = 124
NS = 144

ENGS = ("sync", "gpsimd", "vector", "scalar", "tensor")


class Bld:
    def __init__(self):
        self.recs = {e: [] for e in ENGS}
        self.cnt = {}
        self.lastw = {e: {} for e in ENGS}

    def sem(self, *names):
        for n in names:
            self.cnt[n] = 0

    def op(self, eng, fn, inc=None, amt=1):
        if eng in ("vector", "scalar"):
            self.recs[eng].append(("op", lambda e: e.drain(), None, 0))
        self.recs[eng].append(("op", fn, inc, amt))
        if inc is not None:
            self.cnt[inc] += amt
            return self.cnt[inc]
        return None

    def wait(self, eng, sem, val=None):
        v = self.cnt[sem] if val is None else val
        if v <= 0 or self.lastw[eng].get(sem, -1) >= v:
            return
        self.lastw[eng][sem] = v
        self.recs[eng].append(("wait", sem, v, None))

    def barrier(self):
        for e in ENGS:
            for s in self.cnt:
                self.wait(e, s)

    def replay(self, block, sems):
        def mk(recs):
            def body(eng):
                for r in recs:
                    if r[0] == "wait":
                        eng.wait_ge(sems[r[1]], r[2])
                    else:
                        inst = r[1](eng)
                        if r[2] is not None:
                            inst.then_inc(sems[r[2]], r[3])
            return body
        for ename in ENGS:
            getattr(block, ename)(mk(self.recs[ename]))


def build_nc(stop_after=None, dumps=()):
    nc = bass.Bass("TRN2", target_bir_lowering=False)
    ap_in = nc.dram_tensor("T_fm", [D, NT], F32, kind="ExternalInput").ap()
    ap_mask = nc.dram_tensor("mask_row", [1, NT], F32, kind="ExternalInput").ap()
    ap_smalls = nc.dram_tensor("smalls", [128, NS], F32, kind="ExternalInput").ap()
    ap_w1 = nc.dram_tensor("w1r", [HC, 128, 1024], F32, kind="ExternalInput").ap()
    ap_w2 = nc.dram_tensor("w2r", [DC, 4, 128, 1024], F32, kind="ExternalInput").ap()
    ap_inwT = nc.dram_tensor("in_wr", [16, 128, 1024], F32, kind="ExternalInput").ap()
    ap_pwT = nc.dram_tensor("pw_wT", [D, D], F32, kind="ExternalInput").ap()
    ap_owT = nc.dram_tensor("out_wT", [D, D], F32, kind="ExternalInput").ap()
    ap_winc = nc.dram_tensor("win_cat", [D, 128], F32, kind="ExternalInput").ap()
    ap_woc = nc.dram_tensor("wout_cat", [128, D], F32, kind="ExternalInput").ap()
    ap_out = nc.dram_tensor("out_fm", [D, 2048], F32, kind="ExternalOutput").ap()
    ap_xsc = nc.dram_tensor("x_sc", [DC, 128, NT], F32, kind="Internal").ap()
    ap_gsc = nc.dram_tensor("gate_sc", [DC, 128, NT], F32, kind="Internal").ap()
    ap_dbg = {}
    for (dname, kind, *a) in dumps:
        if kind == "X":
            ap_dbg[dname] = nc.dram_tensor(dname, [D, NT], F32, kind="ExternalOutput").ap()
        elif kind == "pool":
            ap_dbg[dname] = nc.dram_tensor(dname, [128, a[1]], F32, kind="ExternalOutput").ap()
        elif kind == "sm":
            ap_dbg[dname] = nc.dram_tensor(dname, [128, NS], F32, kind="ExternalOutput").ap()
        else:
            ap_dbg[dname] = nc.dram_tensor(dname, [1, a[1]], F32, kind="ExternalOutput").ap()

    b = Bld()
    b.sem("WIN", "WA", "WB", "WC", "WD", "WE", "WF", "WG",
          "AS0", "AS1", "AL0", "AL1", "AL2", "AL3", "O", "T", "V", "S", "G",
          "I0", "I1", "I2", "I3", "I4")

    from contextlib import ExitStack
    with ExitStack() as es:
        X = es.enter_context(nc.sbuf_tensor("X", [128, DC, NT], F32))
        pool = es.enter_context(nc.sbuf_tensor("pool", [128, POOL_W], F32))
        SM = es.enter_context(nc.sbuf_tensor("smalls_s", [128, NS], F32))
        ONESC = es.enter_context(nc.sbuf_tensor("ones_c", [128, 1], F32))
        ONESR = es.enter_context(nc.sbuf_tensor("ones_r", [1, 128], F32))
        PS = [es.enter_context(nc.psum_tensor("ps%d" % i, [128, 512], F32))
              for i in range(8)]
        ps0, ps1, ps2, ps3, ps4, ps5, ps6, ps7 = PS
        sems = {n: es.enter_context(nc.semaphore(n))
                for n in ("WIN", "WA", "WB", "WC", "WD", "WE", "WF", "WG",
                          "AS0", "AS1", "AL0", "AL1", "AL2", "AL3", "O", "T", "V",
                          "S", "G", "I0", "I1", "I2", "I3", "I4")}
        block = es.enter_context(nc.Block())

        def pv(off, w):
            return pool[:, off:off + w]

        def pr(off, w):
            return pool[0:1, off:off + w]

        def xv(k, a, bb):
            return X[:, k, a:bb]

        def sc(col):
            return SM[:, col:col + 1]

        def mm(out, lhsT, rhs, start, stop, inc=None):
            return b.op("tensor",
                        lambda e, o=out, l=lhsT, r=rhs, s=start, p=stop:
                        e.matmul(o, l, r, start=s, stop=p), inc=inc)

        def act(out, in_, func, bias=None, scale=None, inc=None):
            kw = {}
            if bias is not None:
                kw["bias"] = bias
            if scale is not None:
                kw["scale"] = scale
            return b.op("scalar",
                        lambda e, o=out, i=in_, f=func, k=dict(kw):
                        e.activation(out=o, in_=i, func=f, **k), inc=inc)

        def tt(out, i0, i1, opr, inc=None):
            return b.op("vector",
                        lambda e, o=out, a0=i0, a1=i1, op=opr:
                        e.tensor_tensor(out=o, in0=a0, in1=a1, op=op), inc=inc)

        def ts(out, i0, s1, s2, op0, op1=None, inc=None):
            if op1 is None:
                return b.op("vector",
                            lambda e, o=out, a0=i0, x1=s1, x2=s2, p0=op0:
                            e.tensor_scalar(out=o, in0=a0, scalar1=x1, scalar2=x2, op0=p0),
                            inc=inc)
            return b.op("vector",
                        lambda e, o=out, a0=i0, x1=s1, x2=s2, p0=op0, p1=op1:
                        e.tensor_scalar(out=o, in0=a0, scalar1=x1, scalar2=x2, op0=p0, op1=p1),
                        inc=inc)

        def stt(out, i0, scal, i1, op0, op1, inc=None):
            return b.op("vector",
                        lambda e, o=out, a0=i0, s_=scal, a1=i1, p0=op0, p1=op1:
                        e.scalar_tensor_tensor(out=o, in0=a0, scalar=s_, in1=a1, op0=p0, op1=p1),
                        inc=inc)

        def dma(eng, out, in_, inc):
            return b.op(eng, lambda e, o=out, i=in_: e.dma_start(out=o, in_=i),
                        inc=inc, amt=16)

        dma("sync", SM[:, :], ap_smalls, "WIN")
        for pi, (psa, psb) in enumerate(FT):
            dma("sync", X[:, :, psa:psb],
                ap_in[:, psa:psb].rearrange("(k p) t -> p k t", p=128), "I%d" % pi)
        w_input = b.cnt["WIN"]
        w_smalls = w_input
        b.op("gpsimd", lambda e: e.memset(ONESC[:, :], 1.0), inc="G")
        b.op("gpsimd", lambda e: e.memset(ONESR[:, :], 1.0), inc="G")
        g_ones = b.cnt["G"]

        b.wait("scalar", "WIN", w_smalls)
        b.wait("vector", "WIN", w_smalls)
        act(SM[:, SB2H:SB2H + 8], SM[:, SB2:SB2 + 8], AF.Copy, scale=0.5, inc="S")
        P = SPRM
        act(sc(P + 0), sc(SLOGDT), AF.Exp)
        act(sc(P + 1), sc(SLOGA), AF.Exp, inc="S")
        b.wait("vector", "S")
        tt(sc(P + 2), sc(P + 1), sc(P + 0), ALU.mult, inc="V")
        b.wait("scalar", "V")
        act(sc(P + 3), sc(P + 2), AF.Exp, scale=-1.0, inc="S")
        b.wait("vector", "S")
        ts(sc(P + 5), sc(P + 1), 1e-6, 0.0, ALU.is_lt, ALU.add)
        ts(sc(P + 6), sc(P + 1), -1.0, 0.0, ALU.mult, ALU.add)
        ts(sc(P + 7), sc(P + 5), -1.0, 1.0, ALU.mult, ALU.add)
        ts(sc(P + 4), sc(P + 3), 1.0, -1.0, ALU.mult, ALU.add)
        tt(sc(P + 8), sc(P + 6), sc(P + 7), ALU.mult)
        tt(sc(P + 9), sc(P + 8), sc(P + 5), ALU.add)
        b.op("vector", lambda e: e.reciprocal(out=sc(P + 10), in_=sc(P + 9)))
        tt(sc(P + 11), sc(P + 4), sc(P + 10), ALU.mult)
        tt(sc(P + 12), sc(P + 5), sc(P + 0), ALU.mult)
        tt(sc(P + 13), sc(P + 11), sc(P + 7), ALU.mult)
        tt(sc(P + 14), sc(P + 13), sc(P + 12), ALU.add)
        tt(sc(P + 15), sc(P + 14), sc(SBP), ALU.mult, inc="V")
        COL_ABAR, COL_BBAR = P + 3, P + 15

        class LNPipe:
            def __init__(self, rw, sq_off, kind, banks=None):
                self.rw, self.sq, self.kind = rw, sq_off, kind
                self.bk = banks if banks is not None else (ps4, ps5, ps5, ps6, ps7)
                self.sq_tmarks = {}
                self.rows_smark = 0
                self.rows_done_s = 0
                self.rows_done_v = 0
                self.bcast_tmark = 0
                self.w = 0

            def R(self, i):
                return pr(self.rw + i * 512, self.w)

            def stats(self, ts_, te_, ps_free_v=0, in_sem=None):
                self.w = te_ - ts_
                w = self.w
                p_sum, p_sq = self.bk[0], self.bk[1]
                b.wait("tensor", "S", self.rows_smark)
                b.wait("tensor", "V", ps_free_v)
                b.wait("tensor", "G", g_ones)
                if in_sem is not None:
                    b.wait("tensor", in_sem, 16)
                    b.wait("scalar", in_sem, 16)
                for k in range(DC):
                    mm(p_sum[0:1, :w], ONESC[:, :], xv(k, ts_, te_), k == 0, k == DC - 1,
                       inc="T" if k == DC - 1 else None)
                for k in range(DC):
                    par = k % 2
                    b.wait("scalar", "T", self.sq_tmarks.get(par, 0))
                    act(pv(self.sq + par * 512, w), xv(k, ts_, te_), AF.Square, inc="S")
                    b.wait("tensor", "S")
                    mm(p_sq[0:1, :w], ONESC[:, :], pv(self.sq + par * 512, w), k == 0,
                       k == DC - 1, inc="T")
                    self.sq_tmarks[par] = b.cnt["T"]
                self.stats_tmark = b.cnt["T"]

            def rows(self):
                w = self.w
                R = self.R
                b.wait("scalar", "T", self.stats_tmark)
                b.wait("scalar", "V")
                act(R(0), self.bk[0][0:1, :w], AF.Copy, scale=1.0 / D)
                act(R(1), self.bk[1][0:1, :w], AF.Copy, scale=1.0 / D)
                act(R(2), R(0), AF.Square, inc="S")
                self.rows_smark = b.cnt["S"]
                b.wait("vector", "S")
                tt(R(3), R(1), R(2), ALU.subtract, inc="V")
                b.wait("scalar", "V")
                act(R(5), R(3), AF.Copy, bias=1e-5, inc="S")
                b.wait("vector", "S")
                b.op("vector", lambda e, o=R(4), i=R(5):
                     e.reciprocal(out=o, in_=i), inc="V")
                if self.kind == "ln2":
                    tt(R(1), R(3), R(4), ALU.mult, inc="V")
                b.wait("scalar", "V")
                act(R(4), R(4), AF.Sqrt, inc="S")
                if self.kind == "ln2":
                    act(R(5), R(1), AF.Copy, bias=1e-6, inc="S")
                    b.wait("vector", "S")
                    b.op("vector", lambda e, o=R(2), i=R(5):
                         e.reciprocal(out=o, in_=i), inc="V")
                    b.wait("scalar", "V")
                    act(R(2), R(2), AF.Sqrt, inc="S")
                    b.wait("vector", "S")
                    tt(R(6), R(4), R(2), ALU.mult, inc="V")
                self.rows_done_s = b.cnt["S"]
                self.rows_done_v = b.cnt["V"]

            def bcast(self, ps_free_v, extra_s=0):
                w = self.w
                R = self.R
                b.wait("tensor", "S", self.rows_done_s)
                b.wait("tensor", "V", self.rows_done_v)
                b.wait("tensor", "V", ps_free_v)
                if extra_s:
                    b.wait("tensor", "S", extra_s)
                mm(self.bk[2][:, :w], ONESR[:, :], R(0), True, True)
                mm(self.bk[3][:, :w], ONESR[:, :], R(4), True, True, inc="T")
                if self.kind == "ln2":
                    mm(self.bk[4][:, :w], ONESR[:, :], R(6), True, True, inc="T")
                self.bcast_tmark = b.cnt["T"]

        class Ffn:
            def __init__(self, lnx_off, lnx_stride, ht_off, wb1_off, wb2_off,
                         bt_off, ln):
                self.o = (lnx_off, lnx_stride, ht_off, wb1_off, wb2_off, bt_off)
                self.ln = ln
                self.w1_marks = {}
                self.w2_marks = {}
                self.lnx_tmark = {}
                self.v_bevac = {}

            def norm_emit(self, i, ts_, te_):
                lnx_o, lnx_st = self.o[0], self.o[1]
                w = te_ - ts_
                lnx = lnx_o + (i % 2) * lnx_st
                ln = self.ln
                b.wait("vector", "T", ln.bcast_tmark)
                b.wait("vector", "T", self.lnx_tmark.get(i % 2, 0))
                for k in range(DC):
                    tt(pv(lnx + k * w, w), xv(k, ts_, te_), ln.bk[2][:, :w],
                       ALU.subtract)
                    tt(pv(lnx + k * w, w), pv(lnx + k * w, w), ln.bk[3][:, :w],
                       ALU.mult, inc="V")
                self.v_norm_by[i] = b.cnt["V"]

            def run(self, tiles, emit_out=None, xguards=None, in_sems=None,
                    tail_hook=None):
                lnx_o, lnx_st, ht_o, wb1_o, wb2_o, bt_o = self.o
                ln = self.ln
                self.v_norm_by = {}
                ts0, te0 = tiles[0]
                ln.stats(ts0, te0, in_sem=in_sems[0] if in_sems else None)
                ln.rows()
                ln.bcast(0)
                self.norm_emit(0, *tiles[0])
                for i, (ts_, te_) in enumerate(tiles):
                    w = te_ - ts_
                    lnx = lnx_o + (i % 2) * lnx_st
                    nxt = tiles[i + 1] if i + 1 < len(tiles) else None
                    v_norm = self.v_norm_by[i]
                    s_evac = {}
                    for m in range(HC):
                        par = m % 2
                        wsem = "WA" if par == 0 else "WB"
                        b.wait("sync", "T", self.w1_marks.get(par, 0))
                        dma("sync", pv(wb1_o + par * 1024, 1024), ap_w1[m], wsem)
                        b.wait("tensor", wsem, b.cnt[wsem])
                        b.wait("tensor", "V", v_norm)
                        b.wait("tensor", "S", s_evac.get(par, 0))
                        pa = PS[par]
                        for k in range(DC):
                            mm(pa[:, :w], pv(wb1_o + par * 1024 + k * 128, 128),
                               pv(lnx + k * w, w), k == 0, k == DC - 1,
                               inc="T" if k == DC - 1 else None)
                        self.w1_marks[par] = b.cnt["T"]
                        b.wait("scalar", "T")
                        act(pv(ht_o + m * w, w), pa[:, :w], AF.Gelu, bias=sc(SB1 + m),
                            inc="S")
                        s_evac[par] = b.cnt["S"]
                    self.lnx_tmark[i % 2] = b.cnt["T"]
                    s_all_evac = b.cnt["S"]
                    self.s_a_evac = s_all_evac
                    if nxt is not None:
                        xg = xguards[i + 1] if xguards else 0
                        ln.stats(nxt[0], nxt[1], ps_free_v=max(v_norm, xg),
                                 in_sem=in_sems[i + 1] if in_sems else None)
                        ln.rows()
                    s_bevac = {}
                    for e in range(DC):
                        pb = PS[2 + e % 2]
                        for jg in range(4):
                            par = (e * 4 + jg) % 2
                            wsem = "WC" if par == 0 else "WD"
                            b.wait("sync", "T", self.w2_marks.get(par, 0))
                            dma("sync", pv(wb2_o + par * 1024, 1024), ap_w2[e, jg],
                                wsem)
                            b.wait("tensor", wsem, b.cnt[wsem])
                            if jg == 0:
                                b.wait("tensor", "S", s_all_evac)
                                b.wait("tensor", "S", s_bevac.get(e % 2, 0))
                                b.wait("tensor", "V", v_norm)
                            for j in range(8):
                                jj = jg * 8 + j
                                mm(pb[:, :w], pv(wb2_o + par * 1024 + j * 128, 128),
                                   pv(ht_o + jj * w, w), jj == 0, jj == HC - 1,
                                   inc="T" if j == 7 else None)
                            self.w2_marks[par] = b.cnt["T"]
                        b.wait("scalar", "T")
                        b.wait("scalar", "V", self.v_bevac.get(e % 2, 0))
                        act(pv(bt_o + (e % 2) * 512, w), pb[:, :w], AF.Copy,
                            scale=0.5, inc="S")
                        s_bevac[e % 2] = b.cnt["S"]
                        b.wait("vector", "S")
                        stt(xv(e, ts_, te_), pv(bt_o + (e % 2) * 512, w),
                            sc(SB2H + e), xv(e, ts_, te_), ALU.add, ALU.add, inc="V")
                        self.v_bevac[e % 2] = b.cnt["V"]
                        if emit_out is not None:
                            emit_out(e, ts_, te_)
                        if e == 0 and nxt is not None:
                            ln.bcast(v_norm)
                            self.norm_emit(i + 1, nxt[0], nxt[1])
                        if nxt is None and tail_hook is not None:
                            tail_hook(e)

        b.wait("tensor", "WIN", w_input)
        b.wait("vector", "WIN", w_input)
        b.wait("scalar", "WIN", w_input)
        ln1 = LNPipe(P1_RW, P1_SQ, "ln")
        ffn1 = Ffn(P1_LNX, 8 * FTW, P1_HT, P1_WB1, P1_WB2, P1_BT, ln1)

        lnp2pre = LNPipe(P2_RW, P2_SQ2, "ln2", banks=(ps0, ps1, ps4, ps6, ps7))
        lnp2 = LNPipe(P2_RW, P2_SQ2, "ln2", banks=(ps2, ps3, ps4, ps6, ps7))
        tn_marks = {}
        pp = {"mask_smark": 0, "we_pre": 0, "w_mask": 0}

        def chain_emit(i, ln):
            ts_, te_ = FT[i]
            w = te_ - ts_
            b.wait("vector", "T", ln.bcast_tmark)
            for k in range(DC):
                tmp = pv(P2_TMP + (k % 2) * 512, w)
                tt(tmp, xv(k, ts_, te_), ps4[:, :w], ALU.subtract)
                stt(pv(P2_TN + i * 8 * FTW + k * FTW, w), tmp, sc(SNS + k),
                    ps7[:, :w], ALU.mult, ALU.mult, inc="V")
                tt(tmp, tmp, ps6[:, :w], ALU.mult)
                tt(xv(k, ts_, te_), tmp, xv(k, ts_, te_), ALU.add, inc="V")
            tn_marks[i] = b.cnt["V"]

        def prepass_hook(e):
            v_last = ffn1.v_norm_by[len(FT) - 1]
            if e == 0:
                dma("sync", pv(P2_IPW0, 1024), ap_inwT[0], "WE")
                pp["we_pre"] = b.cnt["WE"]
                dma("sync", pr(MASKBC, NT), ap_mask, "WIN")
                pp["w_mask"] = b.cnt["WIN"]
            elif e == 1:
                b.wait("tensor", "WIN", pp["w_mask"])
                sm = 0
                for i, (ts_, te_) in enumerate(FT):
                    w = te_ - ts_
                    b.wait("tensor", "S", sm)
                    mm(ps7[:, :w], ONESR[:, :], pr(MASKBC + ts_, w), True, True,
                       inc="T")
                    b.wait("scalar", "T")
                    act(pv(MASKBC + ts_, w), ps7[:, :w], AF.Copy, inc="S")
                    sm = b.cnt["S"]
                pp["mask_smark"] = sm
            elif e == 2:
                b.wait("tensor", "S", ffn1.s_a_evac)
                lnp2pre.stats(FT[0][0], FT[0][1], ps_free_v=v_last)
            elif e == 3:
                lnp2pre.rows()
            elif e == 4:
                lnp2pre.bcast(v_last, extra_s=pp["mask_smark"])
            elif e == 5:
                chain_emit(0, lnp2pre)

        ffn1.run(FT, in_sems=["I0", "I1", "I2", "I3", "I4"],
                 tail_hook=prepass_hook if stop_after != "p1" else None)
        p1_s_final = b.cnt["S"]

        def phase_done(name):
            if stop_after == name:
                emit_dumps()
                return True
            return False

        def emit_dumps():
            b.barrier()
            for (dname, kind, *a) in dumps:
                if kind == "X":
                    dma("gpsimd", ap_dbg[dname].rearrange("(k p) t -> p k t", p=128),
                        X[:, :, :], "O")
                elif kind == "pool":
                    dma("gpsimd", ap_dbg[dname], pv(a[0], a[1]), "O")
                elif kind == "sm":
                    dma("gpsimd", ap_dbg[dname], SM[:, :], "O")
                else:
                    dma("gpsimd", ap_dbg[dname], pr(a[0], a[1]), "O")
            b.wait("gpsimd", "O")

        if not phase_done("p1"):
            b.wait("sync", "T")
            b.wait("tensor", "S", lnp2pre.rows_smark)
            b.wait("vector", "S", pp["mask_smark"])
            ip_wmark = {}
            ip_smark = {}
            ip_amark = {}
            pww_we = {}
            for ti, (ts_, te_) in enumerate(FT):
                w = te_ - ts_
                for m in range(16):
                    par = m % 2
                    wsem = "WA" if par == 0 else "WB"
                    if m == 0:
                        b.wait("tensor", "WE", pp["we_pre"])
                        wbase = P2_IPW0
                    else:
                        b.wait("sync", "T", ip_wmark.get(par, 0))
                        dma("sync", pv(P2_WBI + par * 1024, 1024), ap_inwT[m], wsem)
                        b.wait("tensor", wsem, b.cnt[wsem])
                        wbase = P2_WBI + par * 1024
                    pa = PS[par]
                    spar = par
                    pm = ip_smark.get(spar)
                    if pm is not None:
                        b.wait("tensor", pm[0], pm[1])
                    b.wait("tensor", "V", tn_marks[ti])
                    for k in range(DC):
                        mm(pa[:, :w], pv(wbase + k * 128, 128),
                           pv(P2_TN + ti * 8 * FTW + k * FTW, w), k == 0, k == DC - 1,
                           inc="T" if k == DC - 1 else None)
                    ip_wmark[par] = b.cnt["T"]
                    if ti + 1 < len(FT):
                        if m == 0:
                            b.wait("tensor", "S", p1_s_final)
                            lnp2.stats(FT[ti + 1][0], FT[ti + 1][1])
                        elif m == 1:
                            lnp2.rows()
                        elif m == 2:
                            lnp2.bcast(tn_marks[ti])
                        elif m == 13:
                            chain_emit(ti + 1, lnp2)
                    else:
                        if 7 <= m <= 14:
                            k3 = m - 7
                            dma("sync", pv(P3_PWW + k3 * 1024, 1024),
                                ap_pwT[k3 * 128:(k3 + 1) * 128, :], "WE")
                            pww_we[k3] = b.cnt["WE"]
                        elif m == 15:
                            dma("sync", pv(P3_WIN, 1024).rearrange(
                                "p (k n) -> p k n", k=DC),
                                ap_winc.rearrange("(k p) n -> p k n", p=128), "WE")
                            pww_we["win"] = b.cnt["WE"]
                    scr = pv(P2_SCR + spar * 512, w)
                    asem = "AS0" if spar == 0 else "AS1"
                    b.wait("scalar", asem, ip_amark.get(spar, 0))
                    b.wait("vector", asem, ip_amark.get(spar, 0))
                    if m < 8:
                        b.wait("vector", "T")
                        stt(scr, pa[:, :w], sc(SINB + m), pv(MASKBC + ts_, w),
                            ALU.add, ALU.mult, inc="V")
                        ip_smark[spar] = ("V", b.cnt["V"])
                        b.wait("gpsimd", "V")
                        dma("gpsimd", ap_xsc[m, :, ts_:te_], scr, asem)
                    else:
                        b.wait("scalar", "T")
                        act(scr, pa[:, :w], AF.Sigmoid, bias=sc(SINB + m), inc="S")
                        ip_smark[spar] = ("S", b.cnt["S"])
                        b.wait("gpsimd", "S")
                        dma("gpsimd", ap_gsc[m - 8, :, ts_:te_], scr, asem)
                    ip_amark[spar] = b.cnt[asem]

        if stop_after == "p2" or (stop_after == "p1"):
            pass
        if stop_after == "p2":
            emit_dumps()
        if stop_after not in ("p1", "p2"):
            b.barrier()
            b.op("vector", lambda e: e.memset(pv(DATA1, 1), 0.0))
            b.op("vector", lambda e: e.memset(pv(DATA1 + NT - 1, 1), 0.0), inc="V")
            xs_vmark = {}
            xc_tmark = {}
            ev_smark = 0
            d1_vmark = 0
            for si, (a_, b_) in enumerate(PW):
                w = b_ - a_
                for k in range(DC):
                    par = (si * DC + k) % 4
                    lsem = "AL%d" % par
                    b.wait("gpsimd", "V", xs_vmark.get(par, 0))
                    dma("gpsimd", pv(P3_XS + par * (FTW + 4), w + 2),
                        ap_xsc[k, :, a_ - 1:b_ + 1], lsem)
                    xc = pv(P3_XC + par * (FTW + 4), w)
                    b.wait("vector", lsem, b.cnt[lsem])
                    b.wait("vector", "T", xc_tmark.get(par, 0))
                    xs = lambda sh, par=par, w=w: pv(P3_XS + par * (FTW + 4) + sh, w)
                    ts(xc, xs(0), sc(SDW + k * 3 + 0), sc(SDWB + k), ALU.mult, ALU.add)
                    stt(xc, xs(1), sc(SDW + k * 3 + 1), xc, ALU.mult, ALU.add)
                    stt(xc, xs(2), sc(SDW + k * 3 + 2), xc, ALU.mult, ALU.add, inc="V")
                    xs_vmark[par] = b.cnt["V"]
                    b.wait("tensor", "V")
                    if k == 0:
                        b.wait("tensor", "V", ev_smark)
                        b.wait("tensor", "V", d1_vmark)
                    b.wait("tensor", "WE", pww_we[k])
                    for e in range(DC):
                        mm(PS[e][:, :w], pv(P3_PWW + k * 1024 + e * 128, 128), xc,
                           k == 0, k == DC - 1,
                           inc="T" if (e == DC - 1) else None)
                    xc_tmark[par] = b.cnt["T"]
                t_pw = b.cnt["T"]
                b.wait("vector", "T", t_pw)
                b.wait("tensor", "WE", pww_we["win"])
                for e in range(DC):
                    ts(pv(P3_XP + e * 512, w), PS[e][:, :w], sc(SPWB + e), 0.0,
                       ALU.add, ALU.add, inc="V")
                    b.wait("tensor", "V")
                    mm(ps0[:, :w], pv(P3_WIN + e * 128, 128), pv(P3_XP + e * 512, w),
                       e == 0, e == DC - 1, inc="T" if e == DC - 1 else None)
                ev_smark = b.cnt["V"]
                b.wait("vector", "T")
                stt(pv(DATA1 + a_, w), ps0[:, :w], sc(COL_BBAR), pv(MASKBC + a_, w),
                    ALU.mult, ALU.mult, inc="V")
                d1_vmark = b.cnt["V"]

        if stop_after == "p3":
            emit_dumps()
        if stop_after not in ("p1", "p2", "p3"):
            b.barrier()
            dma("sync", pv(P5_WOC, 1024), ap_woc, "WG")
            w_woc = b.cnt["WG"]
            for k in range(DC):
                dma("sync", pv(P5_OW + k * 1024, 1024), ap_owT[k * 128:(k + 1) * 128, :], "WF")
            w_p5 = b.cnt["WF"]
            ts(pv(ABARBC, NT), pv(DATA1, NT), 0.0, sc(COL_ABAR), ALU.mult, ALU.add,
               inc="V")
            FWC = [(1, BT[0][1])] + [BT[i] for i in (1, 2, 3)]
            BWC = {3: BT[3], 2: BT[2], 1: BT[1], 0: (C0, BT[0][1])}

            def fscan(a, bb, init):
                b.op("vector", lambda e, a=a, bb=bb, init=init: e.tensor_tensor_scan(
                    out=pool[0:64, HS + a:HS + bb],
                    data0=pool[0:64, ABARBC + a:ABARBC + bb],
                    data1=pool[0:64, DATA1 + a:DATA1 + bb],
                    initial=init, op0=ALU.mult, op1=ALU.add), inc="V")

            def bscan(a, bb, init):
                b.op("vector", lambda e, a=a, bb=bb, init=init: e.tensor_tensor_scan(
                    out=pool[64:128, HS + bb - 1:HS + a - 1:-1],
                    data0=pool[64:128, ABARBC + bb - 1:ABARBC + a - 1:-1],
                    data1=pool[64:128, DATA1 + bb - 1:DATA1 + a - 1:-1],
                    initial=init, op0=ALU.mult, op1=ALU.add), inc="V")

            ys_mark = {}

            def ys_emit(ti):
                tsb, teb = BT[ti]
                r0 = tsb - C0
                wb = teb - tsb
                bw_a, bw_b = BWC[ti]
                bscan(bw_a, bw_b, pool[64:128, HS + bw_b:HS + bw_b + 1])
                ts(pool[0:64, YS + r0:YS + r0 + wb], pool[0:64, HS + tsb:HS + teb],
                   SM[0:64, SCP:SCP + 1], 0.0, ALU.mult, ALU.add)
                ts(pool[64:128, YS + r0:YS + r0 + wb], pool[64:128, HS + tsb:HS + teb],
                   SM[64:128, SCP:SCP + 1], 0.0, ALU.mult, ALU.add, inc="V")
                ys_mark[ti] = b.cnt["V"]

            for j, (a, bb) in enumerate(FWC):
                fscan(a, bb, 0.0 if j == 0 else pool[0:64, HS + a - 1:HS + a])
            bscan(C1, NT - 1, 0.0)
            ys_emit(3)
            b.wait("tensor", "WG", w_woc)
            g_amark = {}
            g_tmark = {}
            yg_tmark = 0
            o_vmark = {}
            p5_tile_vmark = {}
            for tcnt, ti in enumerate([3, 2, 1, 0]):
                ts_, te_ = BT[ti]
                w = te_ - ts_
                r0 = ts_ - C0
                b.wait("tensor", "V", ys_mark[ti])
                for e in range(DC):
                    par = e % 2
                    gpar = (ti * DC + e) % 2
                    lsem = "AL0" if gpar == 0 else "AL1"
                    b.wait("gpsimd", "V", g_tmark.get(gpar, 0))
                    dma("gpsimd", pv(P5_GT + gpar * 512, w), ap_gsc[e, :, ts_:te_], lsem)
                    amark = b.cnt[lsem]
                    b.wait("tensor", "V", g_tmark.get(("ps", par), 0))
                    b.wait("tensor", "T", yg_tmark)
                    mm(PS[par][:, :w], pv(P5_WOC + e * 128, 128),
                       pool[:, YS + r0:YS + r0 + w], True, True, inc="T")
                    b.wait("vector", "T")
                    b.wait("vector", lsem, amark)
                    tt(pv(P5_YG + e * 512, w), PS[par][:, :w], pv(P5_GT + gpar * 512, w),
                       ALU.mult, inc="V")
                    g_tmark[("ps", par)] = b.cnt["V"]
                    g_tmark[gpar] = b.cnt["V"]
                v_yg = b.cnt["V"]
                if tcnt + 1 < 4:
                    ys_emit([3, 2, 1, 0][tcnt + 1])
                for e2 in range(DC):
                    pb = PS[2 + e2 % 2]
                    b.wait("tensor", "WF", w_p5)
                    b.wait("tensor", "V", v_yg)
                    b.wait("tensor", "V", o_vmark.get(e2 % 2, 0))
                    for k in range(DC):
                        mm(pb[:, :w], pv(P5_OW + k * 1024 + e2 * 128, 128),
                           pv(P5_YG + k * 512, w), k == 0, k == DC - 1,
                           inc="T" if k == DC - 1 else None)
                    if e2 == DC - 1:
                        yg_tmark = b.cnt["T"]
                    b.wait("vector", "T")
                    stt(xv(e2, ts_, te_), pb[:, :w], sc(SOUTB + e2),
                        xv(e2, ts_, te_), ALU.add, ALU.add, inc="V")
                    o_vmark[e2 % 2] = b.cnt["V"]
                    if e2 == DC - 1:
                        p5_tile_vmark[ti] = b.cnt["V"]

        if stop_after == "p5":
            emit_dumps()
        if stop_after is None:
            for eng in ("sync", "gpsimd"):
                for s in b.cnt:
                    b.wait(eng, s)
            b.wait("tensor", "V", p5_tile_vmark[3])
            b.wait("scalar", "V", p5_tile_vmark[3])
            b.wait("vector", "V", p5_tile_vmark[3])
            ln3 = LNPipe(P6_RW, P6_SQ, "ln", banks=(ps6, ps7, ps5, ps4, None))
            ffn3 = Ffn(P6_LNX, 4096, P6_HT, P6_WB1, P6_WB2, P6_BT, ln3)

            def emit_out(e, ts_, te_):
                b.wait("gpsimd", "V")
                dma("gpsimd", ap_out[e * 128:(e + 1) * 128, ts_ - C0:te_ - C0],
                    xv(e, ts_, te_), "O")

            ffn3.run(BT[::-1], emit_out=emit_out,
                     xguards=[p5_tile_vmark[i] for i in (3, 2, 1, 0)])
            b.wait("gpsimd", "O")

        b.replay(block, sems)
    return nc


def _stage_core(T, c):
    bb, half = c // 2, c % 2
    seq = T[bb]
    pad = np.zeros((C0, D), np.float32)
    if half == 0:
        ext = np.concatenate([pad, seq[:C1]], 0)
        mrow = np.ones((1, NT), np.float32)
        mrow[0, :C0] = 0.0
    else:
        ext = np.concatenate([seq[4096 - C1:], pad], 0)
        mrow = np.ones((1, NT), np.float32)
        mrow[0, C1:] = 0.0
    return np.ascontiguousarray(ext.T), mrow


def _stage_shared(i):
    f32 = np.float32
    g = lambda k: np.ascontiguousarray(np.asarray(i[k], f32))
    w1 = g("L0_ffn_w1")
    b1 = g("L0_ffn_b1")
    w2 = g("L0_ffn_w2")
    b2 = g("L0_ffn_b2")
    w1r = np.ascontiguousarray(
        w1.reshape(8, 128, 32, 128).transpose(2, 1, 0, 3).reshape(32, 128, 1024))
    w2r = np.ascontiguousarray(
        w2.reshape(4, 8, 128, 8, 128).transpose(3, 0, 2, 1, 4).reshape(8, 4, 128, 1024))
    in_wT = np.ascontiguousarray(g("L0_attn_in_w").T)
    in_wr = np.ascontiguousarray(
        in_wT.reshape(8, 128, 16, 128).transpose(2, 1, 0, 3).reshape(16, 128, 1024))
    pw_wT = np.ascontiguousarray(g("L0_attn_pw_w").T)
    out_wT = np.ascontiguousarray(g("L0_attn_out_w").T)
    win_cat = np.ascontiguousarray(
        np.concatenate([g("L0_attn_f_win"), g("L0_attn_b_win")], 0).T)
    wout_cat = np.ascontiguousarray(
        np.concatenate([g("L0_attn_f_wout").T, g("L0_attn_b_wout").T], 0))
    sm = np.zeros((128, NS), np.float32)
    sm[:, SB1:SB1 + 32] = b1.reshape(32, 128).T
    sm[:, SB2:SB2 + 8] = b2.reshape(8, 128).T
    sm[:, SINB:SINB + 16] = g("L0_attn_in_b").reshape(16, 128).T
    sm[:, SPWB:SPWB + 8] = g("L0_attn_pw_b").reshape(8, 128).T
    sm[:, SOUTB:SOUTB + 8] = g("L0_attn_out_b").reshape(8, 128).T
    dw = g("L0_attn_dw_w").reshape(D, 3)
    for k in range(DC):
        sm[:, SDW + k * 3:SDW + k * 3 + 3] = dw[k * 128:(k + 1) * 128, :]
    sm[:, SDWB:SDWB + 8] = g("L0_attn_dw_b").reshape(8, 128).T
    sm[:, SNS:SNS + 8] = g("L0_attn_norm_scale").reshape(8, 128).T
    sm[:, SLOGA] = np.concatenate([g("L0_attn_f_logA"), g("L0_attn_b_logA")])
    sm[:, SBP] = np.concatenate([g("L0_attn_f_B"), g("L0_attn_b_B")])
    sm[:, SCP] = np.concatenate([g("L0_attn_f_C"), g("L0_attn_b_C")])
    sm[:, SLOGDT] = np.concatenate(
        [np.full(64, np.float32(np.asarray(i["L0_attn_f_logdt"]))),
         np.full(64, np.float32(np.asarray(i["L0_attn_b_logdt"])))])
    return dict(w1r=w1r, w2r=w2r, in_wr=in_wr, pw_wT=pw_wT, out_wT=out_wT,
                win_cat=win_cat, wout_cat=wout_cat, smalls=sm)


def kernel(**inputs):
    T = np.ascontiguousarray(np.asarray(inputs["L0_attn_T"], np.float32))
    shared = _stage_shared(inputs)
    nc = build_nc()
    in_maps = []
    for c in range(8):
        T_fm, mrow = _stage_core(T, c)
        m = dict(shared)
        m["T_fm"] = T_fm
        m["mask_row"] = mrow
        in_maps.append(m)
    res = run_bass_kernel_spmd(nc, in_maps, core_ids=list(range(8)))
    out = np.empty((4, 4096, 1024), np.float32)
    for c in range(8):
        bb, half = c // 2, c % 2
        out[bb, half * 2048:(half + 1) * 2048, :] = res.results[c]["out_fm"].T
    return out


# version 19
# speedup vs baseline: 1.0920x; 1.0920x over previous
import numpy as np
import concourse.bass as bass
import concourse.mybir as mybir
from concourse.bass_utils import run_bass_kernel_spmd

F32 = mybir.dt.float32
AF = mybir.ActivationFunctionType
ALU = mybir.AluOpType

W_WIN = 50
NT = 2048 + 2 * (W_WIN + 1)
C0, C1 = W_WIN + 1, W_WIN + 1 + 2048
D, DC = 1024, 8
H, HC = 4096, 32
FTW = NT // 5
assert FTW * 5 == NT and FTW % 2 == 0 and FTW <= 512
FT = [(i * FTW, (i + 1) * FTW) for i in range(5)]
BT = [(C0 + i * 512, C0 + (i + 1) * 512) for i in range(4)]
PW = [(1 + i * FTW, min(1 + (i + 1) * FTW, NT - 1)) for i in range(5)]

POOL_W = 35000
MASKBC = 26784
DATA1 = NT
ABARBC = 2 * NT
HS = 3 * NT
YS = 4 * NT
P1_LNX = 0
P1_HT = P1_LNX + 2 * 8 * FTW
P1_WB1 = P1_HT + 32 * FTW
P1_WB2 = P1_WB1 + 2048
P1_SQ = P1_WB2 + 2048
P1_BT = P1_SQ + 1024
P1_RW = P1_BT + 1024
P2_TN = NT
P2_WBI = P2_TN + 8 * NT
P2_TMP = P2_WBI + 2048
P2_SCR = P2_TMP + 1024
P2_SQ2 = 24736
P2_RW = 29856
P2_IPW0 = 33440
P3_PWW = HS
P3_WIN = P3_PWW + 8192
P3_XS = P3_WIN + 1024
P3_XC = P3_XS + 4 * (FTW + 4)
P3_XP = P3_XC + 4 * (FTW + 4)
P5_YG = YS + 2048 + 64
P5_GT = P5_YG + 4096
P5_OW = P5_GT + 1024 + 64
P5_WOC = 33792
P5_TMP = P5_OW
P6_LNX, P6_HT, P6_WB1, P6_WB2, P6_SQ, P6_BT, P6_RW = 0, 8192, 24576, 26624, 28672, 29696, 30720

SB1, SB2, SINB, SPWB, SOUTB, SDW, SDWB, SNS = 0, 32, 40, 56, 64, 72, 96, 104
SLOGA, SBP, SCP, SLOGDT = 112, 113, 114, 115
SB2H = 116
SPRM = 124
NS = 144

ENGS = ("sync", "gpsimd", "vector", "scalar", "tensor")


class Bld:
    def __init__(self):
        self.recs = {e: [] for e in ENGS}
        self.cnt = {}
        self.lastw = {e: {} for e in ENGS}

    def sem(self, *names):
        for n in names:
            self.cnt[n] = 0

    def op(self, eng, fn, inc=None, amt=1):
        if eng in ("vector", "scalar"):
            self.recs[eng].append(("op", lambda e: e.drain(), None, 0))
        self.recs[eng].append(("op", fn, inc, amt))
        if inc is not None:
            self.cnt[inc] += amt
            return self.cnt[inc]
        return None

    def wait(self, eng, sem, val=None):
        v = self.cnt[sem] if val is None else val
        if v <= 0 or self.lastw[eng].get(sem, -1) >= v:
            return
        self.lastw[eng][sem] = v
        self.recs[eng].append(("wait", sem, v, None))

    def barrier(self):
        for e in ENGS:
            for s in self.cnt:
                self.wait(e, s)

    def replay(self, block, sems):
        def mk(recs):
            def body(eng):
                for r in recs:
                    if r[0] == "wait":
                        eng.wait_ge(sems[r[1]], r[2])
                    else:
                        inst = r[1](eng)
                        if r[2] is not None:
                            inst.then_inc(sems[r[2]], r[3])
            return body
        for ename in ENGS:
            getattr(block, ename)(mk(self.recs[ename]))


def build_nc(stop_after=None, dumps=()):
    nc = bass.Bass("TRN2", target_bir_lowering=False)
    ap_in = nc.dram_tensor("T_fm", [D, NT], F32, kind="ExternalInput").ap()
    ap_mask = nc.dram_tensor("mask_row", [1, NT], F32, kind="ExternalInput").ap()
    ap_smalls = nc.dram_tensor("smalls", [128, NS], F32, kind="ExternalInput").ap()
    ap_w1 = nc.dram_tensor("w1r", [HC, 128, 1024], F32, kind="ExternalInput").ap()
    ap_w2 = nc.dram_tensor("w2r", [DC, 4, 128, 1024], F32, kind="ExternalInput").ap()
    ap_inwT = nc.dram_tensor("in_wr", [16, 128, 1024], F32, kind="ExternalInput").ap()
    ap_pwT = nc.dram_tensor("pw_wT", [D, D], F32, kind="ExternalInput").ap()
    ap_owT = nc.dram_tensor("out_wT", [D, D], F32, kind="ExternalInput").ap()
    ap_winc = nc.dram_tensor("win_cat", [D, 128], F32, kind="ExternalInput").ap()
    ap_woc = nc.dram_tensor("wout_cat", [128, D], F32, kind="ExternalInput").ap()
    ap_out = nc.dram_tensor("out_fm", [D, 2048], F32, kind="ExternalOutput").ap()
    ap_xsc = nc.dram_tensor("x_sc", [DC, 128, NT], F32, kind="Internal").ap()
    ap_gsc = nc.dram_tensor("gate_sc", [DC, 128, NT], F32, kind="Internal").ap()
    ap_dbg = {}
    for (dname, kind, *a) in dumps:
        if kind == "X":
            ap_dbg[dname] = nc.dram_tensor(dname, [D, NT], F32, kind="ExternalOutput").ap()
        elif kind == "pool":
            ap_dbg[dname] = nc.dram_tensor(dname, [128, a[1]], F32, kind="ExternalOutput").ap()
        elif kind == "sm":
            ap_dbg[dname] = nc.dram_tensor(dname, [128, NS], F32, kind="ExternalOutput").ap()
        else:
            ap_dbg[dname] = nc.dram_tensor(dname, [1, a[1]], F32, kind="ExternalOutput").ap()

    b = Bld()
    b.sem("WIN", "WA", "WB", "WC", "WD", "WE", "WF", "WG",
          "AS0", "AS1", "AL0", "AL1", "AL2", "AL3", "O", "T", "V", "S", "G",
          "I0", "I1", "I2", "I3", "I4")

    from contextlib import ExitStack
    with ExitStack() as es:
        X = es.enter_context(nc.sbuf_tensor("X", [128, DC, NT], F32))
        pool = es.enter_context(nc.sbuf_tensor("pool", [128, POOL_W], F32))
        SM = es.enter_context(nc.sbuf_tensor("smalls_s", [128, NS], F32))
        ONESC = es.enter_context(nc.sbuf_tensor("ones_c", [128, 1], F32))
        ONESR = es.enter_context(nc.sbuf_tensor("ones_r", [1, 128], F32))
        PS = [es.enter_context(nc.psum_tensor("ps%d" % i, [128, 512], F32))
              for i in range(8)]
        ps0, ps1, ps2, ps3, ps4, ps5, ps6, ps7 = PS
        sems = {n: es.enter_context(nc.semaphore(n))
                for n in ("WIN", "WA", "WB", "WC", "WD", "WE", "WF", "WG",
                          "AS0", "AS1", "AL0", "AL1", "AL2", "AL3", "O", "T", "V",
                          "S", "G", "I0", "I1", "I2", "I3", "I4")}
        block = es.enter_context(nc.Block())

        def pv(off, w):
            return pool[:, off:off + w]

        def pr(off, w):
            return pool[0:1, off:off + w]

        def xv(k, a, bb):
            return X[:, k, a:bb]

        def sc(col):
            return SM[:, col:col + 1]

        def mm(out, lhsT, rhs, start, stop, inc=None):
            return b.op("tensor",
                        lambda e, o=out, l=lhsT, r=rhs, s=start, p=stop:
                        e.matmul(o, l, r, start=s, stop=p), inc=inc)

        def act(out, in_, func, bias=None, scale=None, inc=None):
            kw = {}
            if bias is not None:
                kw["bias"] = bias
            if scale is not None:
                kw["scale"] = scale
            return b.op("scalar",
                        lambda e, o=out, i=in_, f=func, k=dict(kw):
                        e.activation(out=o, in_=i, func=f, **k), inc=inc)

        def tt(out, i0, i1, opr, inc=None):
            return b.op("vector",
                        lambda e, o=out, a0=i0, a1=i1, op=opr:
                        e.tensor_tensor(out=o, in0=a0, in1=a1, op=op), inc=inc)

        def ts(out, i0, s1, s2, op0, op1=None, inc=None):
            if op1 is None:
                return b.op("vector",
                            lambda e, o=out, a0=i0, x1=s1, x2=s2, p0=op0:
                            e.tensor_scalar(out=o, in0=a0, scalar1=x1, scalar2=x2, op0=p0),
                            inc=inc)
            return b.op("vector",
                        lambda e, o=out, a0=i0, x1=s1, x2=s2, p0=op0, p1=op1:
                        e.tensor_scalar(out=o, in0=a0, scalar1=x1, scalar2=x2, op0=p0, op1=p1),
                        inc=inc)

        def stt(out, i0, scal, i1, op0, op1, inc=None):
            return b.op("vector",
                        lambda e, o=out, a0=i0, s_=scal, a1=i1, p0=op0, p1=op1:
                        e.scalar_tensor_tensor(out=o, in0=a0, scalar=s_, in1=a1, op0=p0, op1=p1),
                        inc=inc)

        def dma(eng, out, in_, inc):
            return b.op(eng, lambda e, o=out, i=in_: e.dma_start(out=o, in_=i),
                        inc=inc, amt=16)

        dma("sync", SM[:, :], ap_smalls, "WIN")
        for pi, (psa, psb) in enumerate(FT):
            dma("sync", X[:, :, psa:psb],
                ap_in[:, psa:psb].rearrange("(k p) t -> p k t", p=128), "I%d" % pi)
        w_input = b.cnt["WIN"]
        w_smalls = w_input
        b.op("gpsimd", lambda e: e.memset(ONESC[:, :], 1.0), inc="G")
        b.op("gpsimd", lambda e: e.memset(ONESR[:, :], 1.0), inc="G")
        g_ones = b.cnt["G"]

        b.wait("scalar", "WIN", w_smalls)
        b.wait("vector", "WIN", w_smalls)
        act(SM[:, SB2H:SB2H + 8], SM[:, SB2:SB2 + 8], AF.Copy, scale=0.5, inc="S")
        P = SPRM
        act(sc(P + 0), sc(SLOGDT), AF.Exp)
        act(sc(P + 1), sc(SLOGA), AF.Exp, inc="S")
        b.wait("vector", "S")
        tt(sc(P + 2), sc(P + 1), sc(P + 0), ALU.mult, inc="V")
        b.wait("scalar", "V")
        act(sc(P + 3), sc(P + 2), AF.Exp, scale=-1.0, inc="S")
        b.wait("vector", "S")
        ts(sc(P + 5), sc(P + 1), 1e-6, 0.0, ALU.is_lt, ALU.add)
        ts(sc(P + 6), sc(P + 1), -1.0, 0.0, ALU.mult, ALU.add)
        ts(sc(P + 7), sc(P + 5), -1.0, 1.0, ALU.mult, ALU.add)
        ts(sc(P + 4), sc(P + 3), 1.0, -1.0, ALU.mult, ALU.add)
        tt(sc(P + 8), sc(P + 6), sc(P + 7), ALU.mult)
        tt(sc(P + 9), sc(P + 8), sc(P + 5), ALU.add)
        b.op("vector", lambda e: e.reciprocal(out=sc(P + 10), in_=sc(P + 9)))
        tt(sc(P + 11), sc(P + 4), sc(P + 10), ALU.mult)
        tt(sc(P + 12), sc(P + 5), sc(P + 0), ALU.mult)
        tt(sc(P + 13), sc(P + 11), sc(P + 7), ALU.mult)
        tt(sc(P + 14), sc(P + 13), sc(P + 12), ALU.add)
        tt(sc(P + 15), sc(P + 14), sc(SBP), ALU.mult, inc="V")
        COL_ABAR, COL_BBAR = P + 3, P + 15

        class LNPipe:
            def __init__(self, rw, sq_off, kind, banks=None):
                self.rw, self.sq, self.kind = rw, sq_off, kind
                self.bk = banks if banks is not None else (ps4, ps5, ps5, ps6, ps7)
                self.sq_tmarks = {}
                self.rows_smark = 0
                self.rows_done_s = 0
                self.rows_done_v = 0
                self.bcast_tmark = 0
                self.w = 0

            def R(self, i):
                return pr(self.rw + i * 512, self.w)

            def stats(self, ts_, te_, ps_free_v=0, in_sem=None):
                self.w = te_ - ts_
                w = self.w
                p_sum, p_sq = self.bk[0], self.bk[1]
                b.wait("tensor", "S", self.rows_smark)
                b.wait("tensor", "V", ps_free_v)
                b.wait("tensor", "G", g_ones)
                if in_sem is not None:
                    b.wait("tensor", in_sem, 16)
                    b.wait("scalar", in_sem, 16)
                for k in range(DC):
                    mm(p_sum[0:1, :w], ONESC[:, :], xv(k, ts_, te_), k == 0, k == DC - 1,
                       inc="T" if k == DC - 1 else None)
                for k in range(DC):
                    par = k % 2
                    b.wait("scalar", "T", self.sq_tmarks.get(par, 0))
                    act(pv(self.sq + par * 512, w), xv(k, ts_, te_), AF.Square, inc="S")
                    b.wait("tensor", "S")
                    mm(p_sq[0:1, :w], ONESC[:, :], pv(self.sq + par * 512, w), k == 0,
                       k == DC - 1, inc="T")
                    self.sq_tmarks[par] = b.cnt["T"]
                self.stats_tmark = b.cnt["T"]

            def rows(self):
                w = self.w
                R = self.R
                b.wait("scalar", "T", self.stats_tmark)
                b.wait("scalar", "V")
                act(R(0), self.bk[0][0:1, :w], AF.Copy, scale=1.0 / D)
                act(R(1), self.bk[1][0:1, :w], AF.Copy, scale=1.0 / D)
                act(R(2), R(0), AF.Square, inc="S")
                self.rows_smark = b.cnt["S"]
                b.wait("vector", "S")
                tt(R(3), R(1), R(2), ALU.subtract, inc="V")
                b.wait("scalar", "V")
                act(R(5), R(3), AF.Copy, bias=1e-5, inc="S")
                b.wait("vector", "S")
                b.op("vector", lambda e, o=R(4), i=R(5):
                     e.reciprocal(out=o, in_=i), inc="V")
                if self.kind == "ln2":
                    tt(R(1), R(3), R(4), ALU.mult, inc="V")
                b.wait("scalar", "V")
                act(R(4), R(4), AF.Sqrt, inc="S")
                if self.kind == "ln2":
                    act(R(5), R(1), AF.Copy, bias=1e-6, inc="S")
                    b.wait("vector", "S")
                    b.op("vector", lambda e, o=R(2), i=R(5):
                         e.reciprocal(out=o, in_=i), inc="V")
                    b.wait("scalar", "V")
                    act(R(2), R(2), AF.Sqrt, inc="S")
                    b.wait("vector", "S")
                    tt(R(6), R(4), R(2), ALU.mult, inc="V")
                self.rows_done_s = b.cnt["S"]
                self.rows_done_v = b.cnt["V"]

            def bcast(self, ps_free_v, extra_s=0):
                w = self.w
                R = self.R
                b.wait("tensor", "S", self.rows_done_s)
                b.wait("tensor", "V", self.rows_done_v)
                b.wait("tensor", "V", ps_free_v)
                if extra_s:
                    b.wait("tensor", "S", extra_s)
                mm(self.bk[2][:, :w], ONESR[:, :], R(0), True, True)
                mm(self.bk[3][:, :w], ONESR[:, :], R(4), True, True, inc="T")
                if self.kind == "ln2":
                    mm(self.bk[4][:, :w], ONESR[:, :], R(6), True, True, inc="T")
                self.bcast_tmark = b.cnt["T"]

        class Ffn:
            def __init__(self, lnx_off, lnx_stride, ht_off, wb1_off, wb2_off,
                         bt_off, ln):
                self.o = (lnx_off, lnx_stride, ht_off, wb1_off, wb2_off, bt_off)
                self.ln = ln
                self.w1_marks = {}
                self.w2_marks = {}
                self.lnx_tmark = {}
                self.v_bevac = {}
                self.v_norm_by = {}

            def norm_emit(self, i, ts_, te_):
                lnx_o, lnx_st = self.o[0], self.o[1]
                w = te_ - ts_
                lnx = lnx_o + (i % 2) * lnx_st
                ln = self.ln
                b.wait("vector", "T", ln.bcast_tmark)
                b.wait("vector", "T", self.lnx_tmark.get(i % 2, 0))
                for k in range(DC):
                    tt(pv(lnx + k * w, w), xv(k, ts_, te_), ln.bk[2][:, :w],
                       ALU.subtract)
                    tt(pv(lnx + k * w, w), pv(lnx + k * w, w), ln.bk[3][:, :w],
                       ALU.mult, inc="V")
                self.v_norm_by[i] = b.cnt["V"]

            def run(self, tiles, emit_out=None, xguards=None, in_sems=None,
                    tail_hook=None, pre_done=False):
                lnx_o, lnx_st, ht_o, wb1_o, wb2_o, bt_o = self.o
                ln = self.ln
                if not pre_done:
                    ts0, te0 = tiles[0]
                    ln.stats(ts0, te0, in_sem=in_sems[0] if in_sems else None)
                    ln.rows()
                    ln.bcast(0)
                    self.norm_emit(0, *tiles[0])
                for i, (ts_, te_) in enumerate(tiles):
                    w = te_ - ts_
                    lnx = lnx_o + (i % 2) * lnx_st
                    nxt = tiles[i + 1] if i + 1 < len(tiles) else None
                    v_norm = self.v_norm_by[i]
                    s_evac = {}
                    for m in range(HC):
                        par = m % 2
                        wsem = "WA" if par == 0 else "WB"
                        b.wait("sync", "T", self.w1_marks.get(par, 0))
                        dma("sync", pv(wb1_o + par * 1024, 1024), ap_w1[m], wsem)
                        b.wait("tensor", wsem, b.cnt[wsem])
                        b.wait("tensor", "V", v_norm)
                        b.wait("tensor", "S", s_evac.get(par, 0))
                        pa = PS[par]
                        for k in range(DC):
                            mm(pa[:, :w], pv(wb1_o + par * 1024 + k * 128, 128),
                               pv(lnx + k * w, w), k == 0, k == DC - 1,
                               inc="T" if k == DC - 1 else None)
                        self.w1_marks[par] = b.cnt["T"]
                        b.wait("scalar", "T")
                        act(pv(ht_o + m * w, w), pa[:, :w], AF.Gelu, bias=sc(SB1 + m),
                            inc="S")
                        s_evac[par] = b.cnt["S"]
                    self.lnx_tmark[i % 2] = b.cnt["T"]
                    s_all_evac = b.cnt["S"]
                    self.s_a_evac = s_all_evac
                    if nxt is not None:
                        xg = xguards[i + 1] if xguards else 0
                        ln.stats(nxt[0], nxt[1], ps_free_v=max(v_norm, xg),
                                 in_sem=in_sems[i + 1] if in_sems else None)
                        ln.rows()
                    s_bevac = {}
                    for e in range(DC):
                        pb = PS[2 + e % 2]
                        for jg in range(4):
                            par = (e * 4 + jg) % 2
                            wsem = "WC" if par == 0 else "WD"
                            b.wait("sync", "T", self.w2_marks.get(par, 0))
                            dma("sync", pv(wb2_o + par * 1024, 1024), ap_w2[e, jg],
                                wsem)
                            b.wait("tensor", wsem, b.cnt[wsem])
                            if jg == 0:
                                b.wait("tensor", "S", s_all_evac)
                                b.wait("tensor", "S", s_bevac.get(e % 2, 0))
                                b.wait("tensor", "V", v_norm)
                            for j in range(8):
                                jj = jg * 8 + j
                                mm(pb[:, :w], pv(wb2_o + par * 1024 + j * 128, 128),
                                   pv(ht_o + jj * w, w), jj == 0, jj == HC - 1,
                                   inc="T" if j == 7 else None)
                            self.w2_marks[par] = b.cnt["T"]
                        b.wait("scalar", "T")
                        b.wait("scalar", "V", self.v_bevac.get(e % 2, 0))
                        act(pv(bt_o + (e % 2) * 512, w), pb[:, :w], AF.Copy,
                            scale=0.5, inc="S")
                        s_bevac[e % 2] = b.cnt["S"]
                        b.wait("vector", "S")
                        stt(xv(e, ts_, te_), pv(bt_o + (e % 2) * 512, w),
                            sc(SB2H + e), xv(e, ts_, te_), ALU.add, ALU.add, inc="V")
                        self.v_bevac[e % 2] = b.cnt["V"]
                        if emit_out is not None:
                            emit_out(e, ts_, te_)
                        if e == 0 and nxt is not None:
                            ln.bcast(v_norm)
                            self.norm_emit(i + 1, nxt[0], nxt[1])
                        if nxt is None and tail_hook is not None:
                            tail_hook(e)

        b.wait("tensor", "WIN", w_input)
        b.wait("vector", "WIN", w_input)
        b.wait("scalar", "WIN", w_input)
        ln1 = LNPipe(P1_RW, P1_SQ, "ln")
        ffn1 = Ffn(P1_LNX, 8 * FTW, P1_HT, P1_WB1, P1_WB2, P1_BT, ln1)

        lnp2pre = LNPipe(P2_RW, P2_SQ2, "ln2", banks=(ps0, ps1, ps4, ps6, ps7))
        lnp2 = LNPipe(P2_RW, P2_SQ2, "ln2", banks=(ps2, ps3, ps4, ps6, ps7))
        tn_marks = {}
        pp = {"mask_smark": 0, "we_pre": 0, "w_mask": 0}

        def chain_emit(i, ln):
            ts_, te_ = FT[i]
            w = te_ - ts_
            b.wait("vector", "T", ln.bcast_tmark)
            for k in range(DC):
                tmp = pv(P2_TMP + (k % 2) * 512, w)
                tt(tmp, xv(k, ts_, te_), ps4[:, :w], ALU.subtract)
                stt(pv(P2_TN + i * 8 * FTW + k * FTW, w), tmp, sc(SNS + k),
                    ps7[:, :w], ALU.mult, ALU.mult, inc="V")
                tt(tmp, tmp, ps6[:, :w], ALU.mult)
                tt(xv(k, ts_, te_), tmp, xv(k, ts_, te_), ALU.add, inc="V")
            tn_marks[i] = b.cnt["V"]

        def prepass_hook(e):
            v_last = ffn1.v_norm_by[len(FT) - 1]
            if e == 0:
                dma("sync", pv(P2_IPW0, 1024), ap_inwT[0], "WE")
                pp["we_pre"] = b.cnt["WE"]
                dma("sync", pr(MASKBC, NT), ap_mask, "WIN")
                pp["w_mask"] = b.cnt["WIN"]
            elif e == 1:
                b.wait("tensor", "WIN", pp["w_mask"])
                sm = 0
                for i, (ts_, te_) in enumerate(FT):
                    w = te_ - ts_
                    b.wait("tensor", "S", sm)
                    mm(ps7[:, :w], ONESR[:, :], pr(MASKBC + ts_, w), True, True,
                       inc="T")
                    b.wait("scalar", "T")
                    act(pv(MASKBC + ts_, w), ps7[:, :w], AF.Copy, inc="S")
                    sm = b.cnt["S"]
                pp["mask_smark"] = sm
            elif e == 2:
                b.wait("tensor", "S", ffn1.s_a_evac)
                lnp2pre.stats(FT[0][0], FT[0][1], ps_free_v=v_last)
            elif e == 3:
                lnp2pre.rows()
            elif e == 4:
                lnp2pre.bcast(v_last, extra_s=pp["mask_smark"])
            elif e == 5:
                chain_emit(0, lnp2pre)

        ffn1.run(FT, in_sems=["I0", "I1", "I2", "I3", "I4"],
                 tail_hook=prepass_hook if stop_after != "p1" else None)
        p1_s_final = b.cnt["S"]

        def phase_done(name):
            if stop_after == name:
                emit_dumps()
                return True
            return False

        def emit_dumps():
            b.barrier()
            for (dname, kind, *a) in dumps:
                if kind == "X":
                    dma("gpsimd", ap_dbg[dname].rearrange("(k p) t -> p k t", p=128),
                        X[:, :, :], "O")
                elif kind == "pool":
                    dma("gpsimd", ap_dbg[dname], pv(a[0], a[1]), "O")
                elif kind == "sm":
                    dma("gpsimd", ap_dbg[dname], SM[:, :], "O")
                else:
                    dma("gpsimd", ap_dbg[dname], pr(a[0], a[1]), "O")
            b.wait("gpsimd", "O")

        if not phase_done("p1"):
            b.wait("sync", "T")
            b.wait("tensor", "S", lnp2pre.rows_smark)
            b.wait("vector", "S", pp["mask_smark"])
            ip_wmark = {}
            ip_smark = {}
            ip_amark = {}
            pww_we = {}
            for ti, (ts_, te_) in enumerate(FT):
                w = te_ - ts_
                for m in range(16):
                    par = m % 2
                    wsem = "WA" if par == 0 else "WB"
                    if m == 0:
                        b.wait("tensor", "WE", pp["we_pre"])
                        wbase = P2_IPW0
                    else:
                        b.wait("sync", "T", ip_wmark.get(par, 0))
                        dma("sync", pv(P2_WBI + par * 1024, 1024), ap_inwT[m], wsem)
                        b.wait("tensor", wsem, b.cnt[wsem])
                        wbase = P2_WBI + par * 1024
                    pa = PS[par]
                    spar = par
                    pm = ip_smark.get(spar)
                    if pm is not None:
                        b.wait("tensor", pm[0], pm[1])
                    b.wait("tensor", "V", tn_marks[ti])
                    for k in range(DC):
                        mm(pa[:, :w], pv(wbase + k * 128, 128),
                           pv(P2_TN + ti * 8 * FTW + k * FTW, w), k == 0, k == DC - 1,
                           inc="T" if k == DC - 1 else None)
                    ip_wmark[par] = b.cnt["T"]
                    if ti + 1 < len(FT):
                        if m == 0:
                            b.wait("tensor", "S", p1_s_final)
                            lnp2.stats(FT[ti + 1][0], FT[ti + 1][1])
                        elif m == 1:
                            lnp2.rows()
                        elif m == 2:
                            lnp2.bcast(tn_marks[ti])
                        elif m == 13:
                            chain_emit(ti + 1, lnp2)
                    else:
                        if 7 <= m <= 14:
                            k3 = m - 7
                            dma("sync", pv(P3_PWW + k3 * 1024, 1024),
                                ap_pwT[k3 * 128:(k3 + 1) * 128, :], "WE")
                            pww_we[k3] = b.cnt["WE"]
                        elif m == 15:
                            dma("sync", pv(P3_WIN, 1024).rearrange(
                                "p (k n) -> p k n", k=DC),
                                ap_winc.rearrange("(k p) n -> p k n", p=128), "WE")
                            pww_we["win"] = b.cnt["WE"]
                    scr = pv(P2_SCR + spar * 512, w)
                    asem = "AS0" if spar == 0 else "AS1"
                    b.wait("scalar", asem, ip_amark.get(spar, 0))
                    b.wait("vector", asem, ip_amark.get(spar, 0))
                    if m < 8:
                        b.wait("vector", "T")
                        stt(scr, pa[:, :w], sc(SINB + m), pv(MASKBC + ts_, w),
                            ALU.add, ALU.mult, inc="V")
                        ip_smark[spar] = ("V", b.cnt["V"])
                        b.wait("gpsimd", "V")
                        dma("gpsimd", ap_xsc[m, :, ts_:te_], scr, asem)
                    else:
                        b.wait("scalar", "T")
                        act(scr, pa[:, :w], AF.Sigmoid, bias=sc(SINB + m), inc="S")
                        ip_smark[spar] = ("S", b.cnt["S"])
                        b.wait("gpsimd", "S")
                        dma("gpsimd", ap_gsc[m - 8, :, ts_:te_], scr, asem)
                    ip_amark[spar] = b.cnt[asem]

        if stop_after == "p2" or (stop_after == "p1"):
            pass
        if stop_after == "p2":
            emit_dumps()
        if stop_after not in ("p1", "p2"):
            b.barrier()
            b.op("vector", lambda e: e.memset(pv(DATA1, 1), 0.0))
            b.op("vector", lambda e: e.memset(pv(DATA1 + NT - 1, 1), 0.0), inc="V")
            xs_vmark = {}
            xc_tmark = {}
            ev_smark = 0
            d1_vmark = 0
            for si, (a_, b_) in enumerate(PW):
                w = b_ - a_
                for k in range(DC):
                    par = (si * DC + k) % 4
                    lsem = "AL%d" % par
                    b.wait("gpsimd", "V", xs_vmark.get(par, 0))
                    dma("gpsimd", pv(P3_XS + par * (FTW + 4), w + 2),
                        ap_xsc[k, :, a_ - 1:b_ + 1], lsem)
                    xc = pv(P3_XC + par * (FTW + 4), w)
                    b.wait("vector", lsem, b.cnt[lsem])
                    b.wait("vector", "T", xc_tmark.get(par, 0))
                    xs = lambda sh, par=par, w=w: pv(P3_XS + par * (FTW + 4) + sh, w)
                    ts(xc, xs(0), sc(SDW + k * 3 + 0), sc(SDWB + k), ALU.mult, ALU.add)
                    stt(xc, xs(1), sc(SDW + k * 3 + 1), xc, ALU.mult, ALU.add)
                    stt(xc, xs(2), sc(SDW + k * 3 + 2), xc, ALU.mult, ALU.add, inc="V")
                    xs_vmark[par] = b.cnt["V"]
                    b.wait("tensor", "V")
                    if k == 0:
                        b.wait("tensor", "V", ev_smark)
                        b.wait("tensor", "V", d1_vmark)
                    b.wait("tensor", "WE", pww_we[k])
                    for e in range(DC):
                        mm(PS[e][:, :w], pv(P3_PWW + k * 1024 + e * 128, 128), xc,
                           k == 0, k == DC - 1,
                           inc="T" if (e == DC - 1) else None)
                    xc_tmark[par] = b.cnt["T"]
                t_pw = b.cnt["T"]
                b.wait("vector", "T", t_pw)
                b.wait("tensor", "WE", pww_we["win"])
                for e in range(DC):
                    ts(pv(P3_XP + e * 512, w), PS[e][:, :w], sc(SPWB + e), 0.0,
                       ALU.add, ALU.add, inc="V")
                    b.wait("tensor", "V")
                    mm(ps0[:, :w], pv(P3_WIN + e * 128, 128), pv(P3_XP + e * 512, w),
                       e == 0, e == DC - 1, inc="T" if e == DC - 1 else None)
                ev_smark = b.cnt["V"]
                b.wait("vector", "T")
                stt(pv(DATA1 + a_, w), ps0[:, :w], sc(COL_BBAR), pv(MASKBC + a_, w),
                    ALU.mult, ALU.mult, inc="V")
                d1_vmark = b.cnt["V"]

        if stop_after == "p3":
            emit_dumps()
        if stop_after not in ("p1", "p2", "p3"):
            b.barrier()
            dma("sync", pv(P5_WOC, 1024), ap_woc, "WG")
            w_woc = b.cnt["WG"]
            for k in range(DC):
                dma("sync", pv(P5_OW + k * 1024, 1024), ap_owT[k * 128:(k + 1) * 128, :], "WF")
            w_p5 = b.cnt["WF"]
            ts(pv(ABARBC, NT), pv(DATA1, NT), 0.0, sc(COL_ABAR), ALU.mult, ALU.add,
               inc="V")
            FWC = [(1, BT[0][1])] + [BT[i] for i in (1, 2, 3)]
            BWC = {3: BT[3], 2: BT[2], 1: BT[1], 0: (C0, BT[0][1])}

            def fscan(a, bb, init):
                b.op("vector", lambda e, a=a, bb=bb, init=init: e.tensor_tensor_scan(
                    out=pool[0:64, HS + a:HS + bb],
                    data0=pool[0:64, ABARBC + a:ABARBC + bb],
                    data1=pool[0:64, DATA1 + a:DATA1 + bb],
                    initial=init, op0=ALU.mult, op1=ALU.add), inc="V")

            def bscan(a, bb, init):
                b.op("vector", lambda e, a=a, bb=bb, init=init: e.tensor_tensor_scan(
                    out=pool[64:128, HS + bb - 1:HS + a - 1:-1],
                    data0=pool[64:128, ABARBC + bb - 1:ABARBC + a - 1:-1],
                    data1=pool[64:128, DATA1 + bb - 1:DATA1 + a - 1:-1],
                    initial=init, op0=ALU.mult, op1=ALU.add), inc="V")

            ys_mark = {}

            def ys_emit(ti):
                tsb, teb = BT[ti]
                r0 = tsb - C0
                wb = teb - tsb
                bw_a, bw_b = BWC[ti]
                bscan(bw_a, bw_b, pool[64:128, HS + bw_b:HS + bw_b + 1])
                ts(pool[0:64, YS + r0:YS + r0 + wb], pool[0:64, HS + tsb:HS + teb],
                   SM[0:64, SCP:SCP + 1], 0.0, ALU.mult, ALU.add)
                ts(pool[64:128, YS + r0:YS + r0 + wb], pool[64:128, HS + tsb:HS + teb],
                   SM[64:128, SCP:SCP + 1], 0.0, ALU.mult, ALU.add, inc="V")
                ys_mark[ti] = b.cnt["V"]

            for j, (a, bb) in enumerate(FWC):
                fscan(a, bb, 0.0 if j == 0 else pool[0:64, HS + a - 1:HS + a])
            bscan(C1, NT - 1, 0.0)
            ys_emit(3)
            b.wait("tensor", "WG", w_woc)
            ln3 = LNPipe(P6_RW, P6_SQ, "ln", banks=(ps6, ps7, ps5, ps4, None))
            ffn3 = Ffn(P6_LNX, 4096, P6_HT, P6_WB1, P6_WB2, P6_BT, ln3)
            g_amark = {}
            g_tmark = {}
            yg_tmark = 0
            o_vmark = {}
            p5_tile_vmark = {}
            for tcnt, ti in enumerate([3, 2, 1, 0]):
                ts_, te_ = BT[ti]
                w = te_ - ts_
                r0 = ts_ - C0
                b.wait("tensor", "V", ys_mark[ti])
                for e in range(DC):
                    par = e % 2
                    gpar = (ti * DC + e) % 2
                    lsem = "AL0" if gpar == 0 else "AL1"
                    b.wait("gpsimd", "V", g_tmark.get(gpar, 0))
                    dma("gpsimd", pv(P5_GT + gpar * 512, w), ap_gsc[e, :, ts_:te_], lsem)
                    amark = b.cnt[lsem]
                    b.wait("tensor", "V", g_tmark.get(("ps", par), 0))
                    b.wait("tensor", "T", yg_tmark)
                    mm(PS[par][:, :w], pv(P5_WOC + e * 128, 128),
                       pool[:, YS + r0:YS + r0 + w], True, True, inc="T")
                    b.wait("vector", "T")
                    b.wait("vector", lsem, amark)
                    tt(pv(P5_YG + e * 512, w), PS[par][:, :w], pv(P5_GT + gpar * 512, w),
                       ALU.mult, inc="V")
                    g_tmark[("ps", par)] = b.cnt["V"]
                    g_tmark[gpar] = b.cnt["V"]
                v_yg = b.cnt["V"]
                if tcnt + 1 < 4:
                    ys_emit([3, 2, 1, 0][tcnt + 1])
                for e2 in range(DC):
                    pb = PS[2 + e2 % 2]
                    b.wait("tensor", "WF", w_p5)
                    b.wait("tensor", "V", v_yg)
                    b.wait("tensor", "V", o_vmark.get(e2 % 2, 0))
                    for k in range(DC):
                        mm(pb[:, :w], pv(P5_OW + k * 1024 + e2 * 128, 128),
                           pv(P5_YG + k * 512, w), k == 0, k == DC - 1,
                           inc="T" if k == DC - 1 else None)
                    if e2 == DC - 1:
                        yg_tmark = b.cnt["T"]
                    b.wait("vector", "T")
                    stt(xv(e2, ts_, te_), pb[:, :w], sc(SOUTB + e2),
                        xv(e2, ts_, te_), ALU.add, ALU.add, inc="V")
                    o_vmark[e2 % 2] = b.cnt["V"]
                    if e2 == DC - 1:
                        p5_tile_vmark[ti] = b.cnt["V"]
                    if tcnt == 3 and stop_after is None:
                        if e2 == 0:
                            b.wait("scalar", "V", p5_tile_vmark[3])
                            ln3.stats(BT[3][0], BT[3][1], ps_free_v=v_yg)
                        elif e2 == 1:
                            ln3.rows()
                        elif e2 == 2:
                            ln3.bcast(v_yg)
                        elif e2 == 3:
                            ffn3.norm_emit(0, BT[3][0], BT[3][1])

        if stop_after == "p5":
            emit_dumps()
        if stop_after is None:
            for eng in ("sync", "gpsimd"):
                for s in b.cnt:
                    b.wait(eng, s)
            b.wait("tensor", "V", p5_tile_vmark[3])
            b.wait("scalar", "V", p5_tile_vmark[3])
            b.wait("vector", "V", p5_tile_vmark[3])

            def emit_out(e, ts_, te_):
                b.wait("gpsimd", "V")
                dma("gpsimd", ap_out[e * 128:(e + 1) * 128, ts_ - C0:te_ - C0],
                    xv(e, ts_, te_), "O")

            ffn3.run(BT[::-1], emit_out=emit_out,
                     xguards=[p5_tile_vmark[i] for i in (3, 2, 1, 0)],
                     pre_done=True)
            b.wait("gpsimd", "O")

        b.replay(block, sems)
    return nc


def _stage_core(T, c):
    bb, half = c // 2, c % 2
    seq = T[bb]
    pad = np.zeros((C0, D), np.float32)
    if half == 0:
        ext = np.concatenate([pad, seq[:C1]], 0)
        mrow = np.ones((1, NT), np.float32)
        mrow[0, :C0] = 0.0
    else:
        ext = np.concatenate([seq[4096 - C1:], pad], 0)
        mrow = np.ones((1, NT), np.float32)
        mrow[0, C1:] = 0.0
    return np.ascontiguousarray(ext.T), mrow


def _stage_shared(i):
    f32 = np.float32
    g = lambda k: np.ascontiguousarray(np.asarray(i[k], f32))
    w1 = g("L0_ffn_w1")
    b1 = g("L0_ffn_b1")
    w2 = g("L0_ffn_w2")
    b2 = g("L0_ffn_b2")
    w1r = np.ascontiguousarray(
        w1.reshape(8, 128, 32, 128).transpose(2, 1, 0, 3).reshape(32, 128, 1024))
    w2r = np.ascontiguousarray(
        w2.reshape(4, 8, 128, 8, 128).transpose(3, 0, 2, 1, 4).reshape(8, 4, 128, 1024))
    in_wT = np.ascontiguousarray(g("L0_attn_in_w").T)
    in_wr = np.ascontiguousarray(
        in_wT.reshape(8, 128, 16, 128).transpose(2, 1, 0, 3).reshape(16, 128, 1024))
    pw_wT = np.ascontiguousarray(g("L0_attn_pw_w").T)
    out_wT = np.ascontiguousarray(g("L0_attn_out_w").T)
    win_cat = np.ascontiguousarray(
        np.concatenate([g("L0_attn_f_win"), g("L0_attn_b_win")], 0).T)
    wout_cat = np.ascontiguousarray(
        np.concatenate([g("L0_attn_f_wout").T, g("L0_attn_b_wout").T], 0))
    sm = np.zeros((128, NS), np.float32)
    sm[:, SB1:SB1 + 32] = b1.reshape(32, 128).T
    sm[:, SB2:SB2 + 8] = b2.reshape(8, 128).T
    sm[:, SINB:SINB + 16] = g("L0_attn_in_b").reshape(16, 128).T
    sm[:, SPWB:SPWB + 8] = g("L0_attn_pw_b").reshape(8, 128).T
    sm[:, SOUTB:SOUTB + 8] = g("L0_attn_out_b").reshape(8, 128).T
    dw = g("L0_attn_dw_w").reshape(D, 3)
    for k in range(DC):
        sm[:, SDW + k * 3:SDW + k * 3 + 3] = dw[k * 128:(k + 1) * 128, :]
    sm[:, SDWB:SDWB + 8] = g("L0_attn_dw_b").reshape(8, 128).T
    sm[:, SNS:SNS + 8] = g("L0_attn_norm_scale").reshape(8, 128).T
    sm[:, SLOGA] = np.concatenate([g("L0_attn_f_logA"), g("L0_attn_b_logA")])
    sm[:, SBP] = np.concatenate([g("L0_attn_f_B"), g("L0_attn_b_B")])
    sm[:, SCP] = np.concatenate([g("L0_attn_f_C"), g("L0_attn_b_C")])
    sm[:, SLOGDT] = np.concatenate(
        [np.full(64, np.float32(np.asarray(i["L0_attn_f_logdt"]))),
         np.full(64, np.float32(np.asarray(i["L0_attn_b_logdt"])))])
    return dict(w1r=w1r, w2r=w2r, in_wr=in_wr, pw_wT=pw_wT, out_wT=out_wT,
                win_cat=win_cat, wout_cat=wout_cat, smalls=sm)


def kernel(**inputs):
    T = np.ascontiguousarray(np.asarray(inputs["L0_attn_T"], np.float32))
    shared = _stage_shared(inputs)
    nc = build_nc()
    in_maps = []
    for c in range(8):
        T_fm, mrow = _stage_core(T, c)
        m = dict(shared)
        m["T_fm"] = T_fm
        m["mask_row"] = mrow
        in_maps.append(m)
    res = run_bass_kernel_spmd(nc, in_maps, core_ids=list(range(8)))
    out = np.empty((4, 4096, 1024), np.float32)
    for c in range(8):
        bb, half = c // 2, c % 2
        out[bb, half * 2048:(half + 1) * 2048, :] = res.results[c]["out_fm"].T
    return out
